# version 01
# baseline (speedup 1.0000x reference)
import math
import numpy as np
import ml_dtypes
import concourse.bass as bass
import concourse.mybir as mybir
from concourse.bass_utils import run_bass_kernel_spmd

F32 = mybir.dt.float32
BF16 = mybir.dt.bfloat16
AF = mybir.ActivationFunctionType
ALU = mybir.AluOpType
AX = mybir.AxisListType
BF = ml_dtypes.bfloat16

NCORES = 8
B, S, D = 4, 4096, 2048
TPC = (B * S) // NCORES
RS128 = 1.0 / math.sqrt(128.0)
SQ128 = math.sqrt(128.0)

ENGINES = ("sync", "scalar", "vector", "tensor", "gpsimd")


class Sched:

    def __init__(self, nc):
        self.nc = nc
        self.ops = {e: [] for e in ENGINES}
        self.sems = {}
        self.counts = {}

    def sem(self, name):
        if name not in self.sems:
            self.sems[name] = self.nc._stack.enter_context(self.nc.semaphore(name))
            self.counts[name] = 0
        return self.sems[name]

    def add(self, engine, fn, waits=(), inc=None, inc_n=1, serial=True):
        waits = list(waits)
        if serial and engine in ("vector", "scalar", "gpsimd"):
            last = getattr(self, "_last_" + engine, None)
            if last is not None:
                waits.append(last)
            if inc is None:
                inc = "_auto_" + engine
        tgt = None
        if inc is not None:
            self.sem(inc)
            self.counts[inc] += inc_n
            tgt = self.counts[inc]
        if serial and engine in ("vector", "scalar", "gpsimd"):
            setattr(self, "_last_" + engine, (inc, tgt))
        for w, _v in waits:
            self.sem(w)
        self.ops[engine].append((fn, list(waits), inc, inc_n))
        return tgt

    def dma(self, engine, out, in_, inc, waits=()):
        return self.add(engine, lambda e, o=out, i=in_: e.dma_start(out=o, in_=i),
                        waits=waits, inc=inc, inc_n=16, serial=False)

    def count(self, name):
        self.sem(name)
        return self.counts[name]

    def build(self):
        nc = self.nc
        with nc.Block() as block:
            for ename in ENGINES:
                ops = self.ops[ename]

                def section(eng, _ops=ops):
                    for fn, waits, inc, inc_n in _ops:
                        for (w, v) in waits:
                            eng.wait_ge(self.sems[w], v)
                        inst = fn(eng)
                        if inc is not None:
                            if inst is None:
                                inst = eng.nop()
                            inst.then_inc(self.sems[inc], inc_n)
                getattr(block, ename)(section)


class NcCtx:

    def __init__(self):
        import contextlib
        self.nc = bass.Bass("TRN2", target_bir_lowering=False)
        self.nc._stack = contextlib.ExitStack()

    def sbuf(self, name, shape, dtype):
        return self.nc._stack.enter_context(self.nc.sbuf_tensor(name, shape, dtype))

    def psum(self, name, shape, dtype=F32):
        return self.nc._stack.enter_context(self.nc.psum_tensor(name, shape, dtype))

    def close(self):
        self.nc._stack.close()


def _din(nc, name, shape, dtype=F32):
    return nc.dram_tensor(name, shape, dtype, kind="ExternalInput").ap()


def _dout(nc, name, shape, dtype=F32):
    return nc.dram_tensor(name, shape, dtype, kind="ExternalOutput").ap()


_prog_cache = {}


def cached(key, builder):
    if key not in _prog_cache:
        _prog_cache[key] = builder()
    return _prog_cache[key]


def run(nc, in_maps):
    res = run_bass_kernel_spmd(nc, in_maps, core_ids=list(range(NCORES)))
    return res.results


def build_proj(din, dout, T, w_bf16_in=False, out_f32=False, act=None,
               act_scale=1.0, evac_scale=1.0, bias=False, col_scale=False,
               add_in=False, tb_size=512):
    ctx = NcCtx()
    nc = ctx.nc
    ndc = (din + 127) // 128
    noc = (dout + 127) // 128
    ntb = T // tb_size
    x = _din(nc, "x", [din, T], BF16)
    w = _din(nc, "w", [din, dout], BF16 if w_bf16_in else F32)
    y = _dout(nc, "y", [dout, T], F32 if out_f32 else BF16)
    if bias:
        bvec = _din(nc, "bias", [128, noc], F32)
    if col_scale:
        cs = _din(nc, "cs", [1, T], F32)
        onesf = _din(nc, "onesf", [1, 128], F32)
    if add_in:
        ai = _din(nc, "ai", [dout, T], F32)

    pdim = min(din, 128)
    xs = ctx.sbuf("xs", [pdim, ndc * tb_size], BF16)
    wsb = [ctx.sbuf(f"w{i}", [pdim, ndc * 128], BF16) for i in range(2)]
    osb = [ctx.sbuf(f"o{i}", [128, tb_size], F32 if out_f32 else BF16)
           for i in range(2)]
    aisb = [ctx.sbuf(f"ai{i}", [128, tb_size], F32) for i in range(2)] if add_in else None
    if bias:
        bsb = ctx.sbuf("bsb", [128, noc], F32)
    if col_scale:
        o1f = ctx.sbuf("o1f", [1, 128], F32)
        csrow = ctx.sbuf("csrow", [1, tb_size], F32)
        csb = ctx.sbuf("csb", [128, tb_size], F32)
        pcs = ctx.psum("pcs", [128, tb_size])
    ps = [ctx.psum(f"ps{i}", [128, tb_size]) for i in range(2)]

    sc = Sched(nc)
    if bias:
        sc.dma("sync", bsb[:], bvec[:], inc="lc")
    if col_scale:
        sc.dma("sync", o1f[:], onesf[:], inc="lc")
    lc = sc.count("lc")

    iters = [(tb, oc) for tb in range(ntb) for oc in range(noc)]

    def wslice(i):
        tb, oc = iters[i]
        ocw = min(128, dout - oc * 128)
        return oc, ocw

    wq = ["gpsimd", "scalar"] if w_bf16_in else ["gpsimd", "gpsimd"]
    wload_t = [None] * len(iters)
    pe_t = [None] * len(iters)
    ev_t = [None] * len(iters)
    st_t = [None] * len(iters)
    xload_t = [None] * ntb
    cs_t = [None] * ntb
    csread_t = [None] * ntb

    def load_w(i):
        oc, ocw = wslice(i)
        buf = i % 2
        wv = None
        for dc in range(ndc):
            wv = sc.dma(wq[buf],
                        wsb[buf][:, dc * 128:dc * 128 + ocw],
                        w[dc * pdim:(dc + 1) * pdim, oc * 128:oc * 128 + ocw],
                        waits=[("pe", pe_t[i - 2])] if (i >= 2 and dc == 0) else (),
                        inc=f"lw{buf}")
        wload_t[i] = wv

    def load_x(tb):
        t = None
        for dc in range(ndc):
            wts = []
            if tb > 0 and dc == 0:
                last_i = tb * noc - 1
                wts = [("pe", pe_t[last_i])]
            t = sc.dma("sync", xs[:, dc * tb_size:(dc + 1) * tb_size],
                       x[dc * pdim:(dc + 1) * pdim, tb * tb_size:(tb + 1) * tb_size],
                       waits=wts, inc="lx")
        xload_t[tb] = t

    load_x(0)
    load_w(0)
    if len(iters) > 1:
        load_w(1)

    for i, (tb, oc) in enumerate(iters):
        buf = i % 2
        ocw = min(128, dout - oc * 128)
        if oc == 0 and col_scale:
            t = sc.dma("sync", csrow[:], cs[:, tb * tb_size:(tb + 1) * tb_size],
                       waits=[("cpe", sc.count("cpe"))], inc="lcs")
            sc.add("tensor", lambda e: e.matmul(pcs[:], o1f[:], csrow[:],
                                                start=True, stop=True),
                   waits=[("lcs", t), ("lc", lc)] +
                         ([("csr", csread_t[tb - 1])] if tb > 0 else []),
                   inc="cpe")
            cs_t[tb] = sc.add("vector", lambda e: e.tensor_copy(out=csb[:], in_=pcs[:]),
                              waits=[("cpe", sc.count("cpe"))], inc="ev")
        pb = ps[i % 2]
        mm_waits = [(f"lw{buf}", wload_t[i]), ("lx", xload_t[tb])]
        if i >= 2:
            mm_waits.append(("ev", ev_t[i - 2]))
        for dc in range(ndc):
            def mmf(e, dc=dc, buf=buf, pb=pb, ocw=ocw):
                return e.matmul(pb[0:ocw, :],
                                wsb[buf][:, dc * 128:dc * 128 + ocw],
                                xs[:, dc * tb_size:(dc + 1) * tb_size],
                                start=(dc == 0), stop=(dc == ndc - 1))
            tt = sc.add("tensor", mmf, waits=mm_waits if dc == 0 else (),
                        inc="pe" if dc == ndc - 1 else None)
        pe_t[i] = tt
        if i + 2 < len(iters):
            load_w(i + 2)
        if oc == noc - 1 and tb + 1 < ntb:
            load_x(tb + 1)
        ob = osb[i % 2]
        evw = [("pe", pe_t[i])]
        if i >= 2:
            evw.append((f"st{i % 2}", st_t[i - 2]))
        if act is not None:
            def evf(e, pb=pb, ob=ob, oc=oc, ocw=ocw):
                kw = dict(func=act, scale=act_scale)
                if bias:
                    kw["bias"] = bsb[0:ocw, oc:oc + 1]
                return e.activation(out=ob[0:ocw, :], in_=pb[0:ocw, :], **kw)
            sc.add("scalar", evf, waits=evw + ([("lc", lc)] if bias else []), inc="ev")
        else:
            def evf(e, pb=pb, ob=ob, ocw=ocw):
                if evac_scale == 1.0:
                    return e.tensor_copy(out=ob[0:ocw, :], in_=pb[0:ocw, :])
                return e.tensor_scalar(out=ob[0:ocw, :], in0=pb[0:ocw, :],
                                       scalar1=float(evac_scale), scalar2=None,
                                       op0=ALU.mult)
            sc.add("vector", evf, waits=evw, inc="ev")
        if col_scale:
            sc.add("vector", lambda e, ob=ob, ocw=ocw: e.tensor_tensor(
                out=ob[0:ocw, :], in0=ob[0:ocw, :], in1=csb[0:ocw, :], op=ALU.mult),
                waits=[("ev", sc.count("ev")), ("ev", cs_t[tb])], inc="ev")
            if oc == noc - 1:
                csread_t[tb] = sc.add("sync", lambda e: None,
                                      waits=[("ev", sc.count("ev"))], inc="csr")
        if add_in:
            ab = aisb[i % 2]
            aiw = [("ev", ev_t[i - 2])] if i >= 2 else []
            t = sc.dma("sync", ab[0:ocw, :],
                       ai[oc * 128:oc * 128 + ocw, tb * tb_size:(tb + 1) * tb_size],
                       waits=aiw, inc="lai")
            sc.add("vector", lambda e, ob=ob, ab=ab, ocw=ocw: e.tensor_tensor(
                out=ob[0:ocw, :], in0=ob[0:ocw, :], in1=ab[0:ocw, :], op=ALU.add),
                waits=[("ev", sc.count("ev")), ("lai", t)], inc="ev")
        ev_t[i] = sc.count("ev")
        st_t[i] = sc.dma("sync",
                         y[oc * 128:oc * 128 + ocw, tb * tb_size:(tb + 1) * tb_size],
                         ob[0:ocw, :], waits=[("ev", ev_t[i])], inc=f"st{i % 2}")
    sc.add("gpsimd", lambda e: None,
           waits=[("st0", sc.count("st0")), ("st1", sc.count("st1"))])
    sc.build()
    ctx.close()
    return nc


def proj_launch(x_fm_shards, w_full, **kw):
    din, T = x_fm_shards[0].shape
    dout = w_full.shape[1]
    extra = kw.pop("extra", None)
    key = ("proj", din, dout, T, w_full.dtype == BF,
           kw.get("out_f32", False), str(kw.get("act")), kw.get("act_scale", 1.0),
           kw.get("evac_scale", 1.0), kw.get("bias", False),
           kw.get("col_scale", False), kw.get("add_in", False))
    nc = cached(key, lambda: build_proj(din, dout, T,
                                        w_bf16_in=(w_full.dtype == BF), **kw))
    in_maps = []
    for c in range(NCORES):
        m = {"x": np.ascontiguousarray(x_fm_shards[c]), "w": np.ascontiguousarray(w_full)}
        if extra:
            m.update(extra[c])
        in_maps.append(m)
    out = run(nc, in_maps)
    return [r["y"] for r in out]


def build_ln(T, Dm):
    ctx = NcCtx()
    nc = ctx.nc
    x = _din(nc, "x", [T, Dm], F32)
    y = _dout(nc, "y", [T, Dm], BF16)
    nt = T // 128
    xs = [ctx.sbuf(f"xs{i}", [128, Dm], F32) for i in range(2)]
    sq = ctx.sbuf("sq", [128, Dm], F32)
    st1 = ctx.sbuf("st1", [128, 1], F32)
    st2 = ctx.sbuf("st2", [128, 1], F32)
    mu = ctx.sbuf("mu", [128, 1], F32)
    var = ctx.sbuf("var", [128, 1], F32)
    rst = ctx.sbuf("rst", [128, 1], F32)
    ob = [ctx.sbuf(f"ob{i}", [128, Dm], BF16) for i in range(2)]
    sc = Sched(nc)
    inv = 1.0 / Dm
    for i in range(nt):
        b = i % 2
        t = sc.dma("sync", xs[b][:], x[i * 128:(i + 1) * 128, :],
                   waits=[("v", max(0, sc.count("v") - 1))], inc="lx")
        sc.add("vector", lambda e, b=b: e.tensor_reduce(
            out=st1[:].unsqueeze(2), in_=xs[b][:].unsqueeze(1), axis=AX.X, op=ALU.add),
            waits=[("lx", t)], inc="v0")
        sc.add("scalar", lambda e, b=b: e.activation(out=sq[:], in_=xs[b][:],
                                                     func=AF.Square),
               waits=[("lx", t), ("v0", max(0, sc.count("v0") - 4))], inc="sqd")
        sc.add("vector", lambda e: e.tensor_reduce(
            out=st2[:].unsqueeze(2), in_=sq[:].unsqueeze(1), axis=AX.X, op=ALU.add),
            waits=[("sqd", sc.count("sqd"))], inc="v0")
        sc.add("vector", lambda e: e.tensor_scalar(out=mu[:], in0=st1[:], scalar1=inv,
                                                   scalar2=None, op0=ALU.mult), inc="v0")
        sc.add("vector", lambda e: e.scalar_tensor_tensor(
            out=var[:], in0=mu[:], scalar=-1.0, in1=mu[:], op0=ALU.mult, op1=ALU.mult),
            inc="v0")
        sc.add("vector", lambda e: e.scalar_tensor_tensor(
            out=var[:], in0=st2[:], scalar=inv, in1=var[:], op0=ALU.mult, op1=ALU.add),
            inc="v0")
        sc.add("vector", lambda e: e.tensor_scalar(out=var[:], in0=var[:], scalar1=1e-5,
                                                   scalar2=None, op0=ALU.add), inc="v0")
        v0 = sc.count("v0")
        sc.add("scalar", lambda e: e.activation(out=rst[:], in_=var[:], func=AF.Ln),
               waits=[("v0", v0)], inc="s0")
        sc.add("scalar", lambda e: e.activation(out=rst[:], in_=rst[:], func=AF.Exp,
                                                scale=-0.5), inc="s0")
        s0 = sc.count("s0")
        sc.add("vector", lambda e, b=b: e.tensor_scalar(
            out=ob[b][:], in0=xs[b][:], scalar1=mu[:], scalar2=rst[:],
            op0=ALU.subtract, op1=ALU.mult),
            waits=[("s0", s0), (f"st{b}", sc.count(f"st{b}"))], inc="v")
        sc.dma("sync", y[i * 128:(i + 1) * 128, :], ob[b][:],
               waits=[("v", sc.count("v"))], inc=f"st{b}")
    sc.add("sync", lambda e: None,
           waits=[("st0", sc.count("st0")), ("st1", sc.count("st1"))])
    sc.build()
    ctx.close()
    return nc


def ln_launch(x_tm_shards):
    T, Dm = x_tm_shards[0].shape
    nc = cached(("ln", T, Dm), lambda: build_ln(T, Dm))
    out = run(nc, [{"x": np.ascontiguousarray(s)} for s in x_tm_shards])
    return [r["y"] for r in out]


def build_quant(dpart, hpart):
    ctx = NcCtx()
    nc = ctx.nc
    w = _din(nc, "w", [dpart, hpart], F32)
    gind = _din(nc, "gind", [128, 2], BF16)
    gsel = _din(nc, "gsel", [2, 128], BF16)
    y = _dout(nc, "y", [dpart, hpart], BF16)
    HT = min(512, hpart)
    ndc, nht = dpart // 128, hpart // HT
    wf = [ctx.sbuf(f"wf{i}", [128, HT], F32) for i in range(2)]
    wb = ctx.sbuf("wb", [128, HT], BF16)
    aw = ctx.sbuf("aw", [128, HT], BF16)
    sg = ctx.sbuf("sg", [128, HT], BF16)
    srow = ctx.sbuf("srow", [2, HT], BF16)
    gi = ctx.sbuf("gi", [128, 2], BF16)
    gs = ctx.sbuf("gs", [2, 128], BF16)
    m1 = ctx.sbuf("m1", [128, HT], F32)
    msk = ctx.sbuf("msk", [128, HT], BF16)
    oq = [ctx.sbuf(f"oq{i}", [128, HT], BF16) for i in range(2)]
    pss = ctx.psum("pss", [2, HT])
    psb = ctx.psum("psb", [128, HT])
    sc = Sched(nc)
    sc.dma("sync", gi[:], gind[:], inc="lc")
    sc.dma("sync", gs[:], gsel[:], inc="lc")
    for dc in range(ndc):
        for ht in range(nht):
            b = (dc * nht + ht) % 2
            t = sc.dma("sync", wf[b][:], w[dc * 128:(dc + 1) * 128, ht * HT:(ht + 1) * HT],
                       waits=[("fin", max(0, sc.count("fin") - 1))], inc="lw")
            sc.add("scalar", lambda e, b=b: e.activation(out=aw[:], in_=wf[b][:], func=AF.Abs),
                   waits=[("lw", t), ("pe", sc.count("pe")),
                          ("v", sc.count("v"))], inc="s")
            sc.add("scalar", lambda e, b=b: e.activation(out=sg[:], in_=wf[b][:], func=AF.Sign),
                   inc="s")
            stgt = sc.count("s")
            sc.add("tensor", lambda e: e.matmul(pss[:], gi[:], aw[:], start=True, stop=True),
                   waits=[("s", stgt), ("lc", 32), ("v", max(0, sc.count("v") - 3))], inc="pe")
            sc.add("vector", lambda e: e.tensor_scalar(
                out=srow[:], in0=pss[:], scalar1=0.015625, scalar2=1e-8,
                op0=ALU.mult, op1=ALU.max), waits=[("pe", sc.count("pe"))], inc="v")
            sc.add("tensor", lambda e: e.matmul(psb[:], gs[:], srow[:], start=True, stop=True),
                   waits=[("v", sc.count("v"))], inc="pe")
            pet = sc.count("pe")
            sc.add("vector", lambda e: e.scalar_tensor_tensor(
                out=m1[:], in0=psb[:], scalar=-0.501953125, in1=aw[:],
                op0=ALU.mult, op1=ALU.add), waits=[("pe", pet)], inc="v")
            sc.add("vector", lambda e: e.tensor_scalar(
                out=msk[:], in0=m1[:], scalar1=0.0, scalar2=None, op0=ALU.is_gt), inc="v")
            sc.add("vector", lambda e: e.tensor_tensor(
                out=msk[:], in0=msk[:], in1=sg[:], op=ALU.mult), inc="v")
            ob = oq[b]
            sc.add("vector", lambda e, ob=ob: e.tensor_tensor(
                out=ob[:], in0=msk[:], in1=psb[:], op=ALU.mult),
                waits=[(f"st{b}", sc.count(f"st{b}"))], inc="v")
            sc.dma("sync", y[dc * 128:(dc + 1) * 128, ht * HT:(ht + 1) * HT], ob[:],
                   waits=[("v", sc.count("v"))], inc=f"st{b}")
            sc.add("gpsimd", lambda e: None, waits=[("v", sc.count("v"))], inc="fin")
    sc.add("sync", lambda e: None,
           waits=[("st0", sc.count("st0")), ("st1", sc.count("st1"))])
    sc.build()
    ctx.close()
    return nc


def quant_launch(wT):
    din, dout = wT.shape
    hp = dout // NCORES
    nc = cached(("quant", din, hp), lambda: build_quant(din, hp))
    gind = np.zeros((128, 2), dtype=BF)
    gind[:64, 0] = 1
    gind[64:, 1] = 1
    gsel = np.ascontiguousarray(gind.T)
    maps = [{"w": np.ascontiguousarray(wT[:, c * hp:(c + 1) * hp]),
             "gind": gind, "gsel": gsel} for c in range(NCORES)]
    out = run(nc, maps)
    return np.concatenate([r["y"] for r in out], axis=1)


def build_hmix(T, H):
    ctx = NcCtx()
    nc = ctx.nc
    gu = _din(nc, "gu", [T, 2 * H], BF16)
    y = _dout(nc, "y", [T, H], BF16)
    rs = _dout(nc, "rstd", [T, 1], F32)
    nt = T // 128
    gs = [ctx.sbuf(f"gs{i}", [128, 2 * H], BF16) for i in range(2)]
    sl = ctx.sbuf("sl", [128, H], BF16)
    hh = [ctx.sbuf(f"hh{i}", [128, H], BF16) for i in range(2)]
    h2 = ctx.sbuf("h2", [128, H], BF16)
    ssq = ctx.sbuf("ssq", [128, 1], F32)
    rsb = [ctx.sbuf(f"rsb{i}", [128, 1], F32) for i in range(2)]
    sc = Sched(nc)
    for i in range(nt):
        b = i % 2
        t = sc.dma("sync", gs[b][:], gu[i * 128:(i + 1) * 128, :],
                   waits=[("v", max(0, sc.count("v") - 2))], inc="lg")
        sc.add("scalar", lambda e, b=b: e.activation(out=sl[:], in_=gs[b][:, 0:H],
                                                     func=AF.Silu),
               waits=[("lg", t), ("v", max(0, sc.count("v") - 1))], inc="s")
        sc.add("vector", lambda e, b=b: e.tensor_tensor(
            out=hh[b][:], in0=sl[:], in1=gs[b][:, H:2 * H], op=ALU.mult),
            waits=[("s", sc.count("s")), (f"st{b}", sc.count(f"st{b}"))], inc="v")
        sc.add("scalar", lambda e, b=b: e.activation(out=h2[:], in_=hh[b][:], func=AF.Square),
               waits=[("v", sc.count("v"))], inc="s")
        sc.add("vector", lambda e: e.tensor_reduce(
            out=ssq[:].unsqueeze(2), in_=h2[:].unsqueeze(1), axis=AX.X, op=ALU.add),
            waits=[("s", sc.count("s"))], inc="v2")
        sc.add("vector", lambda e: e.tensor_scalar(
            out=ssq[:], in0=ssq[:], scalar1=1.0 / H, scalar2=1.1920929e-07,
            op0=ALU.mult, op1=ALU.add), inc="v2")
        sc.add("scalar", lambda e, b=b: e.activation(out=rsb[b][:], in_=ssq[:], func=AF.Ln),
               waits=[("v2", sc.count("v2")), (f"st{b}", sc.count(f"st{b}"))], inc="s2")
        sc.add("scalar", lambda e, b=b: e.activation(out=rsb[b][:], in_=rsb[b][:],
                                                     func=AF.Exp, scale=-0.5), inc="s2")
        s2 = sc.count("s2")
        sc.dma("sync", y[i * 128:(i + 1) * 128, :], hh[b][:],
               waits=[("v", sc.count("v"))], inc=f"st{b}")
        sc.dma("sync", rs[i * 128:(i + 1) * 128, :], rsb[b][:],
               waits=[("s2", s2)], inc=f"st{b}")
    sc.add("sync", lambda e: None,
           waits=[("st0", sc.count("st0")), ("st1", sc.count("st1"))])
    sc.build()
    ctx.close()
    return nc


def build_mul2(Dm, T):
    ctx = NcCtx()
    nc = ctx.nc
    a = _din(nc, "a", [Dm, T], BF16)
    bb = _din(nc, "b", [Dm, T], BF16)
    y = _dout(nc, "y", [Dm, T], BF16)
    nd = Dm // 128
    asb = [ctx.sbuf(f"a{i}", [128, T], BF16) for i in range(2)]
    bsb = [ctx.sbuf(f"b{i}", [128, T], BF16) for i in range(2)]
    ob = [ctx.sbuf(f"o{i}", [128, T], BF16) for i in range(2)]
    sc = Sched(nc)
    for i in range(nd):
        bf = i % 2
        w0 = [("v", max(0, sc.count("v") - 1))]
        t1 = sc.dma("sync", asb[bf][:], a[i * 128:(i + 1) * 128, :], waits=w0,
                    inc=f"la{bf}")
        t2 = sc.dma("gpsimd", bsb[bf][:], bb[i * 128:(i + 1) * 128, :], waits=w0,
                    inc=f"lb{bf}")
        sc.add("vector", lambda e, bf=bf: e.tensor_tensor(
            out=ob[bf][:], in0=asb[bf][:], in1=bsb[bf][:], op=ALU.mult),
            waits=[(f"la{bf}", t1), (f"lb{bf}", t2),
                   (f"st{bf}", sc.count(f"st{bf}"))], inc="v")
        sc.dma("sync", y[i * 128:(i + 1) * 128, :], ob[bf][:],
               waits=[("v", sc.count("v"))], inc=f"st{bf}")
    sc.add("sync", lambda e: None,
           waits=[("st0", sc.count("st0")), ("st1", sc.count("st1"))])
    sc.build()
    ctx.close()
    return nc


def build_conv(Cm, T, K=4):
    ctx = NcCtx()
    nc = ctx.nc
    xh = _din(nc, "xh", [Cm, K - 1 + T], BF16)
    cw = _din(nc, "cw", [Cm, K], F32)
    y = _dout(nc, "y", [Cm, T], BF16)
    nch = Cm // 128
    xsb = [ctx.sbuf(f"x{i}", [128, K - 1 + T], BF16) for i in range(2)]
    wsb = ctx.sbuf("w", [128, nch * K], F32)
    acc = [ctx.sbuf(f"acc{i}", [128, T], F32) for i in range(2)]
    ob = [ctx.sbuf(f"o{i}", [128, T], BF16) for i in range(2)]
    sc = Sched(nc)
    sc.dma("gpsimd", wsb[:].rearrange("p (c k) -> p c k", k=K),
           cw[:].rearrange("(c p) k -> p c k", p=128), inc="lwc")
    for i in range(nch):
        b = i % 2
        t = sc.dma("sync", xsb[b][:], xh[i * 128:(i + 1) * 128, :],
                   waits=[("v", max(0, sc.count("v") - 2))], inc="lx")
        for tau in range(K):
            def cstep(e, b=b, i=i, tau=tau):
                if tau == 0:
                    return e.tensor_scalar(out=acc[b][:], in0=xsb[b][:, 0:T],
                                           scalar1=wsb[:, i * K:i * K + 1],
                                           scalar2=None, op0=ALU.mult)
                return e.scalar_tensor_tensor(out=acc[b][:], in0=xsb[b][:, tau:tau + T],
                                              scalar=wsb[:, i * K + tau:i * K + tau + 1],
                                              in1=acc[b][:], op0=ALU.mult, op1=ALU.add)
            sc.add("vector", cstep,
                   waits=[("lx", t), ("lwc", 16), ("s", sc.count("s"))]
                   if tau == 0 else (), inc="v")
        sc.add("scalar", lambda e, b=b: e.activation(out=ob[b][:], in_=acc[b][:],
                                                     func=AF.Silu),
               waits=[("v", sc.count("v")), (f"st{b}", sc.count(f"st{b}"))], inc="s")
        sc.dma("sync", y[i * 128:(i + 1) * 128, :], ob[b][:],
               waits=[("s", sc.count("s"))], inc=f"st{b}")
    sc.add("sync", lambda e: None,
           waits=[("st0", sc.count("st0")), ("st1", sc.count("st1"))])
    sc.build()
    ctx.close()
    return nc


def fm_shards(x_btd):
    out = []
    for c in range(NCORES):
        b, h = divmod(c, 2)
        blk = x_btd[b, h * TPC:(h + 1) * TPC, :]
        out.append(np.ascontiguousarray(blk.T))
    return out


def fm_unshard(shards):
    Dm = shards[0].shape[0]
    x = np.empty((B, S, Dm), dtype=shards[0].dtype)
    for c in range(NCORES):
        b, h = divmod(c, 2)
        x[b, h * TPC:(h + 1) * TPC, :] = shards[c].T
    return x


def tm_shards(x_btd):
    return [np.ascontiguousarray(x_btd[c // 2, (c % 2) * TPC:(c % 2 + 1) * TPC, :])
            for c in range(NCORES)]


def tm_unshard(shards):
    Dm = shards[0].shape[1]
    x = np.empty((B, S, Dm), dtype=shards[0].dtype)
    for c in range(NCORES):
        x[c // 2, (c % 2) * TPC:(c % 2 + 1) * TPC, :] = shards[c]
    return x


def hadamard(n):
    H = np.array([[1.0]], np.float32)
    while H.shape[0] < n:
        H = np.block([[H, H], [H, -H]])
    return np.ascontiguousarray(H)


def build_attn_core(S_, HPC, qblk=512):
    ctx = NcCtx()
    nc = ctx.nc
    q = _din(nc, "q", [HPC * 128, S_], BF16)
    k = _din(nc, "k", [HPC * 128, S_], BF16)
    v = _din(nc, "v", [HPC * 128, S_], BF16)
    g = _din(nc, "g", [HPC * 128, S_], BF16)
    dmask = _din(nc, "dmask", [128, 4 * qblk], BF16)
    ones_b = _din(nc, "ones_b", [128, 1], BF16)
    ones_f = _din(nc, "ones_f", [1, 128], F32)
    idn = _din(nc, "idn", [128, 128], BF16)
    og = _dout(nc, "og", [HPC * 128, S_], BF16)

    nq = S_ // qblk
    nk = S_ // 128
    qs = ctx.sbuf("qs", [128, S_], BF16)
    ks = ctx.sbuf("ks", [128, S_], BF16)
    vs = ctx.sbuf("vs", [128, S_], BF16)
    vt = ctx.sbuf("vt", [128, S_], BF16)
    gs = [ctx.sbuf(f"gs{i}", [128, qblk], BF16) for i in range(2)]
    msk = ctx.sbuf("msk", [128, 4 * qblk], BF16)
    o1b = ctx.sbuf("o1b", [128, 1], BF16)
    o1f = ctx.sbuf("o1f", [1, 128], F32)
    idt = ctx.sbuf("idt", [128, 128], BF16)
    sqs = ctx.sbuf("sqs", [128, S_], BF16)
    nrm = ctx.sbuf("nrm", [1, 2 * S_], F32)
    mx = ctx.sbuf("mx", [1, 2], F32)
    cns = ctx.sbuf("cns", [1, 1], F32)
    cnb = ctx.sbuf("cnb", [128, 1], F32)
    ex = [ctx.sbuf(f"ex{i}", [128, qblk], BF16) for i in range(3)]
    rrow = ctx.sbuf("rrow", [1, qblk], F32)
    rbs = ctx.sbuf("rbs", [128, qblk], F32)
    avn = ctx.sbuf("avn", [128, qblk], BF16)
    oo = [ctx.sbuf(f"oo{i}", [128, qblk], BF16) for i in range(2)]
    pst = [ctx.psum(f"pst{i}", [128, qblk]) for i in range(2)]
    pav = ctx.psum("pav", [128, qblk])
    pcs = ctx.psum("pcs", [1, qblk])
    pg = ctx.psum("pg", [128, qblk])
    pn = ctx.psum("pn", [1, 512])
    pt = ctx.psum("pt_", [128, 1024], BF16)[:, 0:128]

    sc = Sched(nc)
    sc.dma("sync", msk[:], dmask[:], inc="lc")
    sc.dma("sync", o1b[:], ones_b[:], inc="lc")
    sc.dma("sync", o1f[:], ones_f[:], inc="lc")
    sc.dma("sync", idt[:], idn[:], inc="lc")
    lc = sc.count("lc")

    hdone_t = 0
    for h in range(HPC):
        r0 = h * 128
        tq = sc.dma("sync", qs[:], q[r0:r0 + 128, :],
                    waits=[("hdone", hdone_t)], inc="lq")
        tk = sc.dma("gpsimd", ks[:], k[r0:r0 + 128, :],
                    waits=[("hdone", hdone_t)], inc="lk")
        tv = sc.dma("gpsimd", vs[:], v[r0:r0 + 128, :],
                    waits=[("hdone", hdone_t)], inc="lv")
        vtc_t = sc.count("vtc")
        for c in range(nk):
            tp = sc.add("tensor", lambda e, c=c: e.transpose(
                pt[:], vs[:, c * 128:(c + 1) * 128], idt[:]),
                waits=[("lv", tv), ("lc", lc), ("vtc", vtc_t)], inc="vtp")
            vtc_t = sc.add("vector", lambda e, c=c: e.tensor_copy(
                out=vt[:, c * 128:(c + 1) * 128], in_=pt[:]),
                waits=[("vtp", tp)], inc="vtc")
        nev_t = sc.count("nev")
        for src, base, ldw in [(qs, 0, ("lq", tq)), (ks, S_, ("lk", tk))]:
            sqd = sc.add("scalar", lambda e, src=src: e.activation(
                out=sqs[:], in_=src[:], func=AF.Square),
                waits=[ldw, ("npe", sc.count("npe"))], inc="sq")
            for tt in range(S_ // 512):
                np_t = sc.add("tensor", lambda e, tt=tt: e.matmul(
                    pn[:], o1b[:], sqs[:, tt * 512:(tt + 1) * 512],
                    start=True, stop=True),
                    waits=[("sq", sqd), ("nev", nev_t)], inc="npe")
                nev_t = sc.add("vector", lambda e, tt=tt, base=base: e.tensor_copy(
                    out=nrm[:, base + tt * 512:base + (tt + 1) * 512], in_=pn[:]),
                    waits=[("npe", np_t)], inc="nev")
        sc.add("vector", lambda e: e.tensor_reduce(
            out=mx[:, 0:1].unsqueeze(2), in_=nrm[:, 0:S_].unsqueeze(1), axis=AX.X,
            op=ALU.max), waits=[("nev", nev_t)], inc="cv")
        sc.add("vector", lambda e: e.tensor_reduce(
            out=mx[:, 1:2].unsqueeze(2), in_=nrm[:, S_:2 * S_].unsqueeze(1), axis=AX.X,
            op=ALU.max), inc="cv")
        cv1 = sc.add("vector", lambda e: e.tensor_tensor(
            out=cns[:], in0=mx[:, 0:1], in1=mx[:, 1:2], op=ALU.mult), inc="cv")
        cs1 = sc.add("scalar", lambda e: e.activation(out=cns[:], in_=cns[:],
                                                      func=AF.Sqrt),
                     waits=[("cv", cv1)], inc="cs")
        cpe = sc.add("tensor", lambda e: e.matmul(pg[:, 0:1], o1f[:], cns[:],
                                                  start=True, stop=True),
                     waits=[("cs", cs1), ("nv", sc.count("nv"))], inc="cpe")
        cbv = sc.add("vector", lambda e: e.tensor_scalar(
            out=cnb[:], in0=pg[:, 0:1], scalar1=-1.0, scalar2=None, op0=ALU.mult),
            waits=[("cpe", cpe)], inc="cbv")

        flat = []
        for qb in range(nq):
            for kt in range(4 * qb + 4):
                flat.append((qb, kt))
        exp_t = {}
        av_t = {}
        last_norm_read = sc.count("nv")
        fidx = 0
        for qb in range(nq):
            nkt = 4 * qb + 4
            for kt in range(nkt):
                sb = fidx % 2
                eb = fidx % 3
                stw = [("cbv", cbv)]
                if fidx >= 2:
                    stw.append(("ex", exp_t[flat[fidx - 2]]))
                stp = sc.add("tensor", lambda e, kt=kt, qb=qb, sb=sb: e.matmul(
                    pst[sb][:], ks[:, kt * 128:(kt + 1) * 128],
                    qs[:, qb * qblk:(qb + 1) * qblk], start=True, stop=True),
                    waits=stw, inc="stp")
                exw = [("stp", stp)]
                if fidx >= 3:
                    exw.append(("avp", av_t[flat[fidx - 3]]))
                et = sc.add("scalar", lambda e, sb=sb, eb=eb: e.activation(
                    out=ex[eb][:], in_=pst[sb][:], func=AF.Exp, bias=cnb[:]),
                    waits=exw, inc="ex")
                if kt >= 4 * qb:
                    j = kt - 4 * qb
                    et = sc.add("vector", lambda e, eb=eb, j=j: e.tensor_tensor(
                        out=ex[eb][:], in0=ex[eb][:],
                        in1=msk[:, j * qblk:(j + 1) * qblk], op=ALU.mult),
                        waits=[("ex", et)], inc="ex")
                exp_t[(qb, kt)] = et
                avw = [("ex", et), ("vtc", vtc_t)]
                if kt == 0:
                    avw.append(("nv", last_norm_read))
                sc.add("tensor", lambda e, kt=kt, eb=eb, first=(kt == 0),
                       last=(kt == nkt - 1): e.matmul(
                    pav[:], vt[:, kt * 128:(kt + 1) * 128], ex[eb][:],
                    start=first, stop=last), waits=avw, inc="avp")
                av_t[(qb, kt)] = sc.add(
                    "tensor", lambda e, eb=eb, first=(kt == 0),
                    last=(kt == nkt - 1): e.matmul(
                        pcs[:], o1b[:], ex[eb][:], start=first, stop=last),
                    inc="avp")
                fidx += 1
            avp = av_t[(qb, nkt - 1)]
            nv1 = sc.add("vector", lambda e: e.reciprocal(out=rrow[:], in_=pcs[:]),
                         waits=[("avp", avp)], inc="nv")
            rbp = sc.add("tensor", lambda e: e.matmul(pg[:], o1f[:], rrow[:],
                                                      start=True, stop=True),
                         waits=[("nv", nv1)], inc="rbp")
            nv2 = sc.add("vector", lambda e: e.tensor_copy(out=rbs[:], in_=pg[:]),
                         waits=[("rbp", rbp)], inc="nv")
            nv3 = sc.add("vector", lambda e: e.tensor_tensor(
                out=avn[:], in0=pav[:], in1=rbs[:], op=ALU.mult),
                waits=[("nv", nv2)], inc="nv")
            last_norm_read = nv3
            gb = gs[qb % 2]
            tg = sc.dma("gpsimd", gb[:], g[r0:r0 + 128, qb * qblk:(qb + 1) * qblk],
                        waits=[("nv", max(0, sc.count("nv") - 8))],
                        inc=f"lg{qb % 2}")
            ob = oo[qb % 2]
            nv4 = sc.add("vector", lambda e, ob=ob, gb=gb: e.tensor_tensor(
                out=ob[:], in0=avn[:], in1=gb[:], op=ALU.mult),
                waits=[("nv", nv3), (f"lg{qb % 2}", tg),
                       (f"st{qb % 2}", sc.count(f"st{qb % 2}"))], inc="nv")
            sc.dma("sync", og[r0:r0 + 128, qb * qblk:(qb + 1) * qblk], ob[:],
                   waits=[("nv", nv4)], inc=f"st{qb % 2}")
        hdone_t = sc.add("gpsimd", lambda e: None,
                         waits=[("st0", sc.count("st0")), ("st1", sc.count("st1")),
                                ("nv", sc.count("nv")),
                                ("ex", sc.count("ex")), ("sq", sc.count("sq"))],
                         inc="hdone")
    sc.add("sync", lambda e: None,
           waits=[("st0", sc.count("st0")), ("st1", sc.count("st1"))])
    sc.build()
    ctx.close()
    return nc


def attn_core_launch(q_fm, k_fm, v_fm, g_fm):
    HPC = 8
    nc = cached(("attnc", S, HPC), lambda: build_attn_core(S, HPC))
    qblk = 512
    dm = np.zeros((128, 4 * qblk), dtype=BF)
    for j in range(4):
        kk = np.arange(128)[:, None]
        qq = np.arange(qblk)[None, :]
        dm[:, j * qblk:(j + 1) * qblk] = (qq >= (kk + j * 128)).astype(BF)
    ones_b = np.ones((128, 1), dtype=BF)
    ones_f = np.ones((1, 128), dtype=np.float32)
    idn = np.eye(128, dtype=BF)
    maps = []
    for c in range(NCORES):
        b, hh = divmod(c, 2)
        rows = slice(hh * 1024, (hh + 1) * 1024)
        maps.append({
            "q": np.ascontiguousarray(q_fm[b][rows]),
            "k": np.ascontiguousarray(k_fm[b][rows]),
            "v": np.ascontiguousarray(v_fm[b][rows]),
            "g": np.ascontiguousarray(g_fm[b][rows]),
            "dmask": dm, "ones_b": ones_b, "ones_f": ones_f, "idn": idn,
        })
    out = run(nc, maps)
    og = [np.empty((0,))] * B
    res = []
    for c in range(NCORES):
        res.append(out[c]["og"])
    return [np.concatenate([res[2 * b], res[2 * b + 1]], axis=0) for b in range(B)]


def kernel(**ins):
    f32 = np.float32
    x0 = np.ascontiguousarray(ins["L0_attn_x"], dtype=f32)

    xn0_tm = ln_launch(tm_shards(x0))
    xn0 = tm_unshard(xn0_tm)
    xn0_fm = fm_shards(xn0)

    wq = np.ascontiguousarray(ins["L0_attn_q_w"].T)
    wk = np.ascontiguousarray(ins["L0_attn_k_w"].T)
    wv = np.ascontiguousarray(ins["L0_attn_v_w"].T)
    wo = np.ascontiguousarray(ins["L0_attn_o_w"].T)
    q_fm = proj_launch(xn0_fm, wq, evac_scale=RS128)
    k_fm = proj_launch(xn0_fm, wk)
    v_fm = proj_launch(xn0_fm, wv)
    gwT = np.ascontiguousarray(ins["L0_attn_gate_w"].T.astype(f32))
    gbd = np.zeros((D, D), dtype=f32)
    for hh in range(16):
        gbd[hh * 128:(hh + 1) * 128, hh * 128:(hh + 1) * 128] = gwT
    gbias = np.ascontiguousarray(
        np.repeat(ins["L0_attn_gate_b"].astype(f32).reshape(128, 1), 16, axis=1))
    g_fm = proj_launch(q_fm, gbd, act=AF.Sigmoid, act_scale=SQ128, bias=True,
                       extra=[{"bias": gbias}] * NCORES)
    def to_batch(shards):
        return [np.concatenate([shards[2 * b], shards[2 * b + 1]], axis=1)
                for b in range(B)]
    og_b = attn_core_launch(to_batch(q_fm), to_batch(k_fm), to_batch(v_fm),
                            to_batch(g_fm))
    og_fm = [np.ascontiguousarray(og_b[c // 2][:, (c % 2) * TPC:(c % 2 + 1) * TPC])
             for c in range(NCORES)]
    x0_fm = fm_shards(x0)
    r1 = proj_launch(og_fm, wo, out_f32=True, add_in=True,
                     extra=[{"ai": np.ascontiguousarray(x0_fm[c])} for c in range(NCORES)])

    wguT = np.ascontiguousarray(ins["L0_ffn_w_gate_up"].T.astype(f32))
    wpT = np.ascontiguousarray(ins["L0_ffn_w_proj"].T.astype(f32))
    wgu_q = quant_launch(wguT)
    wp_q = quant_launch(wpT)
    gu_fm = proj_launch(xn0_fm, wgu_q)
    nch = cached(("hmix", TPC, 8192), lambda: build_hmix(TPC, 8192))
    hmaps = [{"gu": np.ascontiguousarray(gu_fm[c].T)} for c in range(NCORES)]
    hout = run(nch, hmaps)
    h_fm = [np.ascontiguousarray(hout[c]["y"].T) for c in range(NCORES)]
    rstd = [np.ascontiguousarray(hout[c]["rstd"].reshape(1, TPC)) for c in range(NCORES)]
    onesf = np.ones((1, 128), np.float32)
    x1_fm = proj_launch(h_fm, wp_q, out_f32=True, col_scale=True, add_in=True,
                        extra=[{"cs": rstd[c], "ai": np.ascontiguousarray(r1[c]),
                                "onesf": onesf} for c in range(NCORES)])

    x1 = fm_unshard(x1_fm)
    xn1_tm = ln_launch(tm_shards(x1))
    xn1 = tm_unshard(xn1_tm)
    xn1_fm = fm_shards(xn1)

    wqkvT = np.ascontiguousarray(ins["L1_attn_W_qkv"].T.astype(f32))
    wzT = np.ascontiguousarray(ins["L1_attn_W_z"].T.astype(f32))
    wbaT = np.ascontiguousarray(
        np.concatenate([ins["L1_attn_W_b"], ins["L1_attn_W_a"]], axis=0).T.astype(f32))
    qkv_fm = proj_launch(xn1_fm, wqkvT)
    z_fm = proj_launch(xn1_fm, wzT)
    ba_fm = proj_launch(xn1_fm, wbaT, out_f32=True)
    qkv_b = [np.concatenate([qkv_fm[2 * b], qkv_fm[2 * b + 1]], axis=1) for b in range(B)]
    cwf = np.ascontiguousarray(ins["L1_attn_conv_w"][:, 0, :].astype(f32))
    ncv = cached(("conv", 4096, S), lambda: build_conv(4096, S))
    cmaps = []
    for c in range(NCORES):
        b, hh = divmod(c, 2)
        xh = np.zeros((4096, 3 + S), dtype=BF)
        xh[:, 3:] = qkv_b[b][hh * 4096:(hh + 1) * 4096]
        cmaps.append({"xh": np.ascontiguousarray(xh),
                      "cw": np.ascontiguousarray(cwf[hh * 4096:(hh + 1) * 4096])})
    cout = run(ncv, cmaps)
    cqkv_b = [np.concatenate([cout[2 * b]["y"], cout[2 * b + 1]["y"]], axis=0)
              for b in range(B)]
    ba_b = [np.ascontiguousarray(
        np.concatenate([ba_fm[2 * b], ba_fm[2 * b + 1]], axis=1)) for b in range(B)]
    z_b = [np.ascontiguousarray(
        np.concatenate([z_fm[2 * b], z_fm[2 * b + 1]], axis=1)) for b in range(B)]
    core_tm = delta_launch(cqkv_b, ba_b, z_b,
                           ins["L1_attn_dt_bias"].astype(f32),
                           ins["L1_attn_A_log"].astype(f32),
                           ins["L1_attn_norm_weight"].astype(f32))
    gcore_fm = [np.ascontiguousarray(
        core_tm[c // 2][(c % 2) * TPC:(c % 2 + 1) * TPC, :].T)
        for c in range(NCORES)]
    woutT = np.ascontiguousarray(ins["L1_attn_W_out"].T.astype(f32))
    r2 = proj_launch(gcore_fm, woutT, out_f32=True, add_in=True,
                     extra=[{"ai": np.ascontiguousarray(x1_fm[c])} for c in range(NCORES)])

    w1T = np.ascontiguousarray(ins["L1_ffn_W1"].T.astype(f32))
    w3T = np.ascontiguousarray(ins["L1_ffn_W3"].T.astype(f32))
    w2T = np.ascontiguousarray(ins["L1_ffn_W2"].T.astype(f32))
    sug = proj_launch(xn1_fm, w1T, act=AF.Silu)
    ul = proj_launch(xn1_fm, w3T)
    ncm = cached(("mul2", 8192, TPC), lambda: build_mul2(8192, TPC))
    h1 = run(ncm, [{"a": sug[c], "b": ul[c]} for c in range(NCORES)])
    h1_fm = [h1[c]["y"] for c in range(NCORES)]
    H64 = hadamard(64)
    H128 = hadamard(128)
    hA = [np.ascontiguousarray(h1_fm[c].reshape(128, 64, TPC).transpose(1, 0, 2)
                               .reshape(64, 128 * TPC)) for c in range(NCORES)]
    zA = proj_launch(hA, H64, evac_scale=0.125, tb_size=512)
    hB = [np.ascontiguousarray(zA[c].reshape(64, 128, TPC).transpose(1, 0, 2)
                               .reshape(128, 64 * TPC)) for c in range(NCORES)]
    zB = proj_launch(hB, H128, evac_scale=float(RS128), tb_size=512)
    hf = [np.ascontiguousarray(zB[c].reshape(128, 64, TPC).reshape(8192, TPC))
          for c in range(NCORES)]
    x2_fm = proj_launch(hf, w2T, out_f32=True, add_in=True,
                        extra=[{"ai": np.ascontiguousarray(r2[c])} for c in range(NCORES)])
    return fm_unshard(x2_fm).astype(np.float32)


def build_delta(HPL=4, TBS=512, trunc=""):
    C = 64
    NKH = HPL // 2
    NTB = S // TBS
    NCH = TBS // C
    ctx = NcCtx()
    nc = ctx.nc
    cq = _din(nc, "cq", [NKH * 128, S], BF16)
    ck = _din(nc, "ck", [NKH * 128, S], BF16)
    cv = _din(nc, "cv", [HPL * 128, S], BF16)
    zg = _din(nc, "z", [HPL * 128, S], BF16)
    bax = _din(nc, "bax", [64, S], F32)
    albias = _din(nc, "albias", [64, 1], F32)
    nwt_d = _din(nc, "nwt_in", [128, 128], BF16)
    lowi_d = _din(nc, "lowi_in", [64, 64], F32)
    slown_d = _din(nc, "slown_in", [64, 64], F32)
    uppi_d = _din(nc, "uppi_in", [64, 64], F32)
    idn_d = _din(nc, "idn", [128, 128], BF16)
    onesf_d = _din(nc, "onesf", [1, 128], F32)
    og = _dout(nc, "og", [S, HPL * 128], BF16)

    aug0 = ctx.sbuf("aug0", [64, S], F32)
    aug1 = ctx.sbuf("aug1", [64, S], F32)
    sp0 = ctx.sbuf("sp0", [64, S], F32)
    gfull = ctx.sbuf("gfull", [1, S], F32)
    growb = ctx.sbuf("growb", [1, HPL * TBS], F32)
    browb = ctx.sbuf("browb", [1, HPL * TBS], F32)
    kdrowb = ctx.sbuf("kdrowb", [1, HPL * TBS], F32)
    begrowb = ctx.sbuf("begrowb", [1, HPL * TBS], F32)
    erowb = ctx.sbuf("erowb", [1, HPL * TBS], F32)
    arow = ctx.sbuf("arow", [1, HPL * 64], F32)
    arep = ctx.sbuf("arep", [128, HPL * 64], F32)
    alb = ctx.sbuf("alb", [64, 1], F32)
    negea = ctx.sbuf("negea", [64, 1], F32)
    nwt = ctx.sbuf("nwt", [128, 128], BF16)
    lowi = ctx.sbuf("lowi", [64, 64], F32)
    slown = ctx.sbuf("slown", [64, 64], F32)
    uppi = ctx.sbuf("uppi", [64, 64], F32)
    idt = ctx.sbuf("idt", [128, 128], BF16)
    id64 = ctx.sbuf("id64", [64, 64], BF16)
    o1f = ctx.sbuf("o1f", [1, 128], F32)
    o1b = ctx.sbuf("o1b", [128, 1], BF16)
    qs = ctx.sbuf("qs", [128, NKH * TBS], BF16)
    ks = ctx.sbuf("ks", [128, NKH * TBS], BF16)
    vs = ctx.sbuf("vs", [128, HPL * TBS], BF16)
    zs = ctx.sbuf("zs", [128, HPL * TBS], BF16)
    sqs = ctx.sbuf("sqs", [128, TBS], BF16)
    colbuf = ctx.sbuf("colbuf", [64, 32], F32)
    ktm = ctx.sbuf("ktm", [64, NKH * NCH * 128], BF16)
    KKsb = ctx.sbuf("KKsb", [64, NKH * TBS], BF16)
    atP = ctx.sbuf("atP", [64, NKH * TBS], BF16)
    qg = ctx.sbuf("qg", [128, TBS], BF16)
    bse = ctx.sbuf("bse", [64, TBS], F32)
    expdM = ctx.sbuf("expdM", [64, TBS], F32)
    e2m = ctx.sbuf("e2m", [64, TBS], F32)
    Asb = ctx.sbuf("Asb", [64, TBS], BF16)
    atn = ctx.sbuf("atn", [64, TBS], BF16)
    Rsb = ctx.sbuf("Rsb", [64, NCH * 256], BF16)
    kdsb = ctx.sbuf("kdsb", [64, NCH * 128], BF16)
    Xb = [ctx.sbuf(f"Xb{i}", [64, 256], BF16) for i in range(2)]
    PB = [ctx.sbuf(f"PB{i}", [64, 128], BF16) for i in range(2)]
    B0 = ctx.sbuf("B0", [64, 64], BF16)
    kcf = ctx.sbuf("kcf", [128, 64], BF16)
    vnew = ctx.sbuf("vnew", [64, 128], BF16)
    Ssb = [ctx.sbuf(f"Ssb{i}", [128, 128], BF16) for i in range(HPL)]
    ob = ctx.sbuf("ob", [128, TBS], BF16)
    obn = ctx.sbuf("obn", [128, TBS], BF16)
    ob2 = ctx.sbuf("ob2", [128, TBS], BF16)
    rsd = ctx.sbuf("rsd", [128, TBS // 128], F32)
    zt = ctx.sbuf("zt", [128, TBS], BF16)

    papp = ctx.psum("papp_", [64, 512])[:, 0:256]
    pps_ = ctx.psum("pps_", [128, 512])
    ppow = pps_[0:64, 0:128]
    pdS = pps_[:, 128:256]
    pKK = ctx.psum("pKK", [64, TBS])
    prow = ctx.psum("prow", [64, TBS])
    pvn = ctx.psum("pvn_", [64, 512])[:, 0:128]
    pout = ctx.psum("pout_", [64, 512])[:, 0:128]
    ptr = ctx.psum("ptr_", [128, 1024], BF16)[:, 0:128]
    pbc = ctx.psum("pbc", [128, TBS])

    sc = Sched(nc)
    for dst, src_ in [(nwt, nwt_d), (lowi, lowi_d), (slown, slown_d),
                      (uppi, uppi_d), (idt, idn_d), (o1f, onesf_d), (alb, albias)]:
        sc.dma("sync", dst[:], src_[:], inc="lc")
    t1 = sc.dma("sync", aug1[32:64, :], bax[32:64, :], inc="lc")
    t2 = sc.dma("sync", aug0[0:32, :], bax[0:32, :], inc="lc")
    lc = sc.count("lc")
    sc.add("vector", lambda e: e.tensor_copy(out=id64[:], in_=idt[0:64, 0:64]),
           waits=[("lc", lc)], inc="v")
    sc.add("vector", lambda e: e.memset(o1b[:], 1.0), inc="v")
    for i in range(HPL):
        sc.add("vector", lambda e, i=i: e.memset(Ssb[i][:], 0.0), inc="v")
    if "chunks" in trunc:
        sc.add("vector", lambda e: e.memset(ob[:], 0.1), inc="v")
    sc.add("scalar", lambda e: e.activation(out=aug0[0:32, :], in_=aug0[0:32, :],
                                            func=AF.Sigmoid),
           waits=[("lc", lc)], inc="s")
    sc.add("vector", lambda e: e.tensor_scalar(out=aug1[32:64, :], in0=aug1[32:64, :],
                                               scalar1=alb[32:64, :], scalar2=None,
                                               op0=ALU.add), waits=[("lc", lc)], inc="v")
    sc.add("scalar", lambda e: e.activation(out=sp0[32:64, :], in_=aug1[32:64, :],
                                            func=AF.Abs),
           waits=[("v", sc.count("v"))], inc="s")
    sc.add("scalar", lambda e: e.activation(out=sp0[32:64, :], in_=sp0[32:64, :],
                                            func=AF.Exp, scale=-1.0), inc="s")
    sc.add("vector", lambda e: e.tensor_scalar(out=sp0[32:64, :], in0=sp0[32:64, :],
                                               scalar1=1.0, scalar2=None, op0=ALU.add),
           waits=[("s", sc.count("s"))], inc="v")
    sc.add("scalar", lambda e: e.activation(out=sp0[32:64, :], in_=sp0[32:64, :],
                                            func=AF.Ln),
           waits=[("v", sc.count("v"))], inc="s")
    sc.add("vector", lambda e: e.tensor_scalar(out=aug1[32:64, :], in0=aug1[32:64, :],
                                               scalar1=0.0, scalar2=None, op0=ALU.max),
           inc="v")
    sc.add("vector", lambda e: e.tensor_tensor(out=sp0[32:64, :], in0=sp0[32:64, :],
                                               in1=aug1[32:64, :], op=ALU.add),
           waits=[("s", sc.count("s"))], inc="v")
    sc.add("scalar", lambda e: e.activation(out=negea[0:32, :], in_=alb[0:32, :],
                                            func=AF.Exp), inc="s")
    sc.add("vector", lambda e: e.tensor_scalar(out=negea[0:32, :], in0=negea[0:32, :],
                                               scalar1=-1.0, scalar2=None, op0=ALU.mult),
           waits=[("s", sc.count("s"))], inc="v")
    sc.add("vector", lambda e: e.tensor_scalar(out=aug1[0:32, :], in0=sp0[32:64, :],
                                               scalar1=negea[0:32, :], scalar2=None,
                                               op0=ALU.mult), inc="v")
    for sh in [1, 2, 4, 8, 16, 32]:
        sc.add("vector", lambda e, sh=sh: e.tensor_tensor(
            out=sp0[0:32, :].rearrange("p (c j) -> p c j", j=C)[:, :, sh:C],
            in0=aug1[0:32, :].rearrange("p (c j) -> p c j", j=C)[:, :, 0:C - sh],
            in1=aug1[0:32, :].rearrange("p (c j) -> p c j", j=C)[:, :, sh:C],
            op=ALU.add), inc="v")
        sc.add("vector", lambda e, sh=sh: e.tensor_copy(
            out=aug1[0:32, :].rearrange("p (c j) -> p c j", j=C)[:, :, sh:C],
            in_=sp0[0:32, :].rearrange("p (c j) -> p c j", j=C)[:, :, sh:C]), inc="v")
    gdone = sc.count("v")
    sg_t = sc.count("s")
    for hv in range(HPL):
        tg = sc.dma("sync", gfull[:], aug1[hv:hv + 1, :],
                    waits=[("v", gdone), ("s", sg_t)], inc="gext")
        sg_t = sc.add("scalar", lambda e, hv=hv: e.activation(
            out=arow[:, hv * 64:(hv + 1) * 64], in_=gfull[:, 63::64], func=AF.Exp),
            waits=[("gext", tg)], inc="s")
    ar1 = sc.add("tensor", lambda e: e.matmul(pbc[:, 0:HPL * 64], o1f[:], arow[:],
                                              start=True, stop=True),
                 waits=[("s", sg_t), ("lc", lc)], inc="pe")
    arep_t = sc.add("vector", lambda e: e.tensor_copy(out=arep[:], in_=pbc[:, 0:HPL * 64]),
                    waits=[("pe", ar1)], inc="v")

    last_pbc_read = arep_t
    last_prow_read = 0
    last_KK_read = 0
    S_ready = [arep_t] * HPL
    pe_after_vh = 0
    tbd_t = 0
    for tb in range(NTB):
        t0 = tb * TBS
        for kh_ in range(NKH):
            lq = sc.dma("sync", qs[:, kh_ * TBS:(kh_ + 1) * TBS],
                        cq[kh_ * 128:(kh_ + 1) * 128, t0:t0 + TBS],
                        waits=[("tbd", tbd_t)], inc="lq")
            lk = sc.dma("sync", ks[:, kh_ * TBS:(kh_ + 1) * TBS],
                        ck[kh_ * 128:(kh_ + 1) * 128, t0:t0 + TBS],
                        waits=[("tbd", tbd_t)], inc="lq")
        for hv_ in range(HPL):
            lv = sc.dma("gpsimd", vs[:, hv_ * TBS:(hv_ + 1) * TBS],
                        cv[hv_ * 128:(hv_ + 1) * 128, t0:t0 + TBS],
                        waits=[("tbd", tbd_t)], inc="lv")
            lz = sc.dma("gpsimd", zs[:, hv_ * TBS:(hv_ + 1) * TBS],
                        zg[hv_ * 128:(hv_ + 1) * 128, t0:t0 + TBS],
                        waits=[("tbd", tbd_t)], inc="lv")
        lq = lk = sc.count("lq")
        lv = lz = sc.count("lv")
        rext = []
        for hv in range(HPL):
            o = hv * TBS
            rext.append(sc.dma("sync", growb[:, o:o + TBS], aug1[hv:hv + 1, t0:t0 + TBS],
                               waits=[("rowd", sc.count("rowd")), ("tbd", tbd_t)],
                               inc="rex"))
            rext.append(sc.dma("sync", browb[:, o:o + TBS], aug0[hv:hv + 1, t0:t0 + TBS],
                               waits=[("tbd", tbd_t)], inc="rex"))
        rex_t = sc.count("rex")
        rowd_v = None
        for hv in range(HPL):
            o = hv * TBS
            s1 = sc.add("scalar", lambda e, o=o: e.activation(
                out=erowb[:, o:o + TBS], in_=growb[:, o:o + TBS], func=AF.Exp),
                waits=[("rex", rex_t)], inc="s")
            sc.add("vector", lambda e, o=o: e.tensor_tensor(
                out=begrowb[:, o:o + TBS], in0=browb[:, o:o + TBS],
                in1=erowb[:, o:o + TBS], op=ALU.mult),
                waits=[("s", s1)], inc="v")
            sc.add("vector", lambda e, o=o: e.tensor_tensor(
                out=kdrowb[:, o:o + TBS].rearrange("p (c j) -> p c j", j=C),
                in0=growb[:, o + 63:o + TBS:64].unsqueeze(2).broadcast_to([1, NCH, C]),
                in1=growb[:, o:o + TBS].rearrange("p (c j) -> p c j", j=C),
                op=ALU.subtract), inc="v")
            sc.add("scalar", lambda e, o=o: e.activation(
                out=kdrowb[:, o:o + TBS], in_=kdrowb[:, o:o + TBS], func=AF.Exp),
                waits=[("v", sc.count("v"))], inc="s")
        rowd_v = sc.add("gpsimd", lambda e: None,
                        waits=[("s", sc.count("s")), ("v", sc.count("v")),
                               ("pe", sc.count("pe"))], inc="rowd")

        norm_done = {}
        ktm_done = {}
        for kh in range(NKH):
            r0 = kh * 128
            for which, (buf, ld, extra_scale) in enumerate(
                    [(qs, lq, RS128), (ks, lk, None)]):
                sq1 = sc.add("scalar", lambda e, buf=buf, kh=kh: e.activation(
                    out=sqs[:], in_=buf[:, kh * TBS:(kh + 1) * TBS], func=AF.Square),
                    waits=[("lq", ld), ("pe", sc.count("pe")),
                           ("v", sc.count("v"))], inc="s")
                m1 = sc.add("tensor", lambda e: e.matmul(
                    pbc[0:1, 0:TBS], o1b[:], sqs[:], start=True, stop=True),
                    waits=[("s", sq1), ("v", last_pbc_read)], inc="pe")
                v1 = sc.add("vector", lambda e: e.tensor_scalar(
                    out=bse[0:1, 0:TBS], in0=pbc[0:1, 0:TBS], scalar1=1e-6,
                    scalar2=None, op0=ALU.add), waits=[("pe", m1)], inc="v")
                s2 = sc.add("scalar", lambda e: e.activation(
                    out=bse[0:1, 0:TBS], in_=bse[0:1, 0:TBS], func=AF.Ln),
                    waits=[("v", v1)], inc="s")
                s3 = sc.add("scalar", lambda e: e.activation(
                    out=bse[0:1, 0:TBS], in_=bse[0:1, 0:TBS], func=AF.Exp, scale=-0.5),
                    inc="s")
                if extra_scale is not None:
                    s3 = sc.add("vector", lambda e, es=float(extra_scale): e.tensor_scalar(
                        out=bse[0:1, 0:TBS], in0=bse[0:1, 0:TBS],
                        scalar1=es, scalar2=None, op0=ALU.mult),
                        waits=[("s", s3)], inc="v")
                    s3 = ("v", s3)
                else:
                    s3 = ("s", s3)
                m2 = sc.add("tensor", lambda e: e.matmul(
                    pbc[:, 0:TBS], o1f[:], bse[0:1, 0:TBS], start=True, stop=True),
                    waits=[s3], inc="pe")
                nrm_t = sc.add("vector", lambda e, buf=buf, kh=kh: e.tensor_tensor(
                    out=buf[:, kh * TBS:(kh + 1) * TBS],
                    in0=buf[:, kh * TBS:(kh + 1) * TBS],
                    in1=pbc[:, 0:TBS], op=ALU.mult), waits=[("pe", m2)], inc="v")
                last_pbc_read = nrm_t
                norm_done[(kh, which)] = nrm_t
            kkm = None
            for c in range(NCH):
                kkm = sc.add("tensor", lambda e, c=c, kh=kh: e.matmul(
                    pKK[:, c * C:(c + 1) * C],
                    ks[:, kh * TBS + c * C:kh * TBS + (c + 1) * C],
                    ks[:, kh * TBS + c * C:kh * TBS + (c + 1) * C],
                    start=True, stop=True),
                    waits=[("v", norm_done[(kh, 1)]), ("v", last_KK_read)]
                    if c == 0 else (), inc="pe")
            kk_ev = sc.add("vector", lambda e, kh=kh: e.tensor_copy(
                out=KKsb[:, kh * TBS:(kh + 1) * TBS], in_=pKK[:, 0:TBS]),
                waits=[("pe", kkm)], inc="v")
            apm = None
            for c in range(NCH):
                apm = sc.add("tensor", lambda e, c=c, kh=kh: e.matmul(
                    pKK[:, c * C:(c + 1) * C],
                    ks[:, kh * TBS + c * C:kh * TBS + (c + 1) * C],
                    qs[:, kh * TBS + c * C:kh * TBS + (c + 1) * C],
                    start=True, stop=True),
                    waits=[("v", kk_ev), ("v", norm_done[(kh, 0)])]
                    if c == 0 else (), inc="pe")
            last_KK_read = sc.add("vector", lambda e, kh=kh: e.tensor_copy(
                out=atP[:, kh * TBS:(kh + 1) * TBS], in_=pKK[:, 0:TBS]),
                waits=[("pe", apm)], inc="v")
            for c in range(NCH):
                tp = sc.add("tensor", lambda e, c=c, kh=kh: e.transpose(
                    ptr[0:64, :], ks[:, kh * TBS + c * C:kh * TBS + (c + 1) * C],
                    idt[:]),
                    waits=[("trc", sc.count("trc")), ("v", norm_done[(kh, 1)])],
                    inc="trp")
                td = sc.add("vector", lambda e, kh=kh, c=c: e.tensor_copy(
                    out=ktm[:, (kh * NCH + c) * 128:(kh * NCH + c + 1) * 128],
                    in_=ptr[0:64, :]), waits=[("trp", tp)], inc="trc")
            ktm_done[kh] = sc.count("trc")

        for hv in range(HPL):
            kh = hv // 2
            o = hv * TBS
            cext = None
            for c in range(NCH):
                for j, rowsrc in enumerate([growb, browb, kdrowb, begrowb]):
                    cext = sc.add("tensor", lambda e, c=c, j=j, rowsrc=rowsrc, o=o:
                                  e.matmul(pbc[0:64, (c * 4 + j):(c * 4 + j) + 1],
                                           rowsrc[:, o + c * C:o + (c + 1) * C],
                                           o1f[:, 0:1], start=True, stop=True),
                                  waits=[("rowd", rowd_v), ("v", last_pbc_read)]
                                  if (c == 0 and j == 0) else (), inc="pe")
            cb_t = sc.add("vector", lambda e: e.tensor_copy(
                out=colbuf[:, 0:32], in_=pbc[0:64, 0:32]),
                waits=[("pe", cext), ("pe", pe_after_vh)], inc="v")
            last_pbc_read = cb_t
            m3 = sc.add("tensor", lambda e, o=o: e.matmul(
                pbc[:, 0:TBS], o1f[:], erowb[:, o:o + TBS], start=True, stop=True),
                waits=[("v", last_pbc_read), ("rowd", rowd_v)], inc="pe")
            qg_t = sc.add("vector", lambda e, kh=kh: e.tensor_tensor(
                out=qg[:], in0=qs[:, kh * TBS:(kh + 1) * TBS], in1=pbc[:, 0:TBS],
                op=ALU.mult), waits=[("pe", m3), ("v", norm_done[(kh, 0)])], inc="v")
            last_pbc_read = qg_t
            m4 = sc.add("tensor", lambda e, o=o: e.matmul(
                prow[:, 0:TBS], o1f[:, 0:64], growb[:, o:o + TBS],
                start=True, stop=True),
                waits=[("v", last_prow_read), ("rowd", rowd_v)], inc="pe")
            bv = None
            for c in range(NCH):
                bv = sc.add("vector", lambda e, c=c: e.tensor_scalar(
                    out=bse[:, c * C:(c + 1) * C], in0=prow[:, c * C:(c + 1) * C],
                    scalar1=colbuf[:, c * 4:c * 4 + 1], scalar2=None,
                    op0=ALU.subtract), waits=[("pe", m4), ("v", cb_t)]
                    if c == 0 else (), inc="v")
            last_prow_read = bv
            v5 = sc.add("vector", lambda e: e.tensor_tensor(
                out=expdM[:], in0=bse[:].rearrange("p (c j) -> p c j", j=C),
                in1=lowi[:].unsqueeze(1).broadcast_to([64, NCH, C]), op=ALU.mult),
                inc="v")
            s5 = sc.add("scalar", lambda e: e.activation(out=expdM[:], in_=expdM[:],
                                                         func=AF.Exp, scale=-1.0),
                        waits=[("v", v5)], inc="s")
            v6 = sc.add("vector", lambda e: e.tensor_tensor(
                out=expdM[:].rearrange("p (c j) -> p c j", j=C),
                in0=expdM[:].rearrange("p (c j) -> p c j", j=C),
                in1=slown[:].unsqueeze(1).broadcast_to([64, NCH, C]), op=ALU.mult),
                waits=[("s", s5)], inc="v")
            v7 = sc.add("vector", lambda e: e.tensor_tensor(
                out=e2m[:], in0=bse[:].rearrange("p (c j) -> p c j", j=C),
                in1=uppi[:].unsqueeze(1).broadcast_to([64, NCH, C]), op=ALU.mult),
                inc="v")
            s7 = sc.add("scalar", lambda e: e.activation(out=e2m[:], in_=e2m[:],
                                                         func=AF.Exp),
                        waits=[("v", v7)], inc="s")
            v8 = sc.add("vector", lambda e: e.tensor_tensor(
                out=e2m[:].rearrange("p (c j) -> p c j", j=C),
                in0=e2m[:].rearrange("p (c j) -> p c j", j=C),
                in1=uppi[:].unsqueeze(1).broadcast_to([64, NCH, C]), op=ALU.mult),
                waits=[("s", s7)], inc="v")
            for c in range(NCH):
                sc.add("vector", lambda e, c=c, kh=kh: e.tensor_scalar(
                    out=bse[:, c * C:(c + 1) * C],
                    in0=KKsb[:, kh * TBS + c * C:kh * TBS + (c + 1) * C],
                    scalar1=colbuf[:, c * 4 + 1:c * 4 + 2], scalar2=None,
                    op0=ALU.mult),
                    waits=[("v", v6), ("pe", pe_after_vh)] if c == 0 else (),
                    inc="v")
            av = sc.add("vector", lambda e: e.tensor_tensor(
                out=Asb[:], in0=bse[:], in1=expdM[:], op=ALU.mult), inc="v")
            at_t = sc.add("vector", lambda e, kh=kh: e.tensor_tensor(
                out=atn[:], in0=atP[:, kh * TBS:(kh + 1) * TBS], in1=e2m[:],
                op=ALU.mult), inc="v")
            rb_t = []
            for c in range(NCH):
                tp = sc.add("tensor", lambda e, c=c, hv=hv: e.transpose(
                    ptr[0:64, :], vs[:, hv * TBS + c * C:hv * TBS + (c + 1) * C],
                    idt[:]),
                    waits=[("trc", sc.count("trc")), ("lv", lv),
                           ("pe", pe_after_vh)] if c == 0 else
                          [("trc", sc.count("trc"))], inc="trp")
                rb_t.append(sc.add("vector", lambda e, c=c: e.tensor_scalar(
                    out=Rsb[:, c * 256:c * 256 + 128], in0=ptr[0:64, :],
                    scalar1=colbuf[:, c * 4 + 1:c * 4 + 2], scalar2=None,
                    op0=ALU.mult), waits=[("trp", tp)], inc="trc"))
                sc.add("vector", lambda e, c=c, kh=kh: e.tensor_scalar(
                    out=Rsb[:, c * 256 + 128:c * 256 + 256],
                    in0=ktm[:, (kh * NCH + c) * 128:(kh * NCH + c + 1) * 128],
                    scalar1=colbuf[:, c * 4 + 3:c * 4 + 4], scalar2=None,
                    op0=ALU.mult), waits=[("trc", ktm_done[kh])], inc="v")
                sc.add("vector", lambda e, c=c, kh=kh: e.tensor_scalar(
                    out=kdsb[:, c * 128:(c + 1) * 128],
                    in0=ktm[:, (kh * NCH + c) * 128:(kh * NCH + c + 1) * 128],
                    scalar1=colbuf[:, c * 4 + 2:c * 4 + 3], scalar2=None,
                    op0=ALU.mult), inc="v")
            prep_v = sc.add("gpsimd", lambda e: None,
                            waits=[("v", sc.count("v")), ("trc", sc.count("trc"))],
                            inc="prep")
            ob_prev = None
            for c in (range(NCH) if "chunks" not in trunc else []):
                cgl = tb * NCH + c
                tp = sc.add("tensor", lambda e, c=c: e.transpose(
                    ptr[0:64, 0:64], Asb[:, c * C:(c + 1) * C], id64[:]),
                    waits=[("prep", prep_v), ("trc", sc.count("trc"))], inc="trp")
                b0t = sc.add("vector", lambda e: e.tensor_copy(
                    out=B0[:], in_=ptr[0:64, 0:64]), waits=[("trp", tp)], inc="trc")
                x_ready = None
                pb_ready = b0t
                pb_prev = None
                Pcur, Bcur = Asb[:, c * C:(c + 1) * C], B0[:]
                for k in (range(6) if "solve" not in trunc else []):
                    rhs = (Rsb[:, c * 256:(c + 1) * 256] if k == 0 else Xb[(k + 1) % 2][:])
                    w1 = [("prep", prep_v)] if k == 0 else [("v", x_ready)]
                    a1 = sc.add("tensor", lambda e, rhs=rhs: e.matmul(
                        papp[:], id64[:], rhs, start=True, stop=False),
                        waits=w1, inc="pe")
                    if k == 0:
                        w2 = [("trc", pb_ready)]
                    else:
                        w2 = [("v", pb_ready)] if pb_ready is not None else []
                    a2 = sc.add("tensor", lambda e, rhs=rhs, Bcur=Bcur: e.matmul(
                        papp[:], Bcur, rhs, start=False, stop=True),
                        waits=w2, inc="pe")
                    x_ready = sc.add("vector", lambda e, k=k: e.tensor_copy(
                        out=Xb[k % 2][:], in_=papp[:]), waits=[("pe", a2)], inc="v")
                    if k < 5:
                        p1 = sc.add("tensor", lambda e, Pcur=Pcur, Bcur=Bcur: e.matmul(
                            ppow[:, 0:64], Bcur, Pcur, start=True, stop=True),
                            waits=[("v", pb_prev)] if pb_prev is not None else (),
                            inc="pe")
                        p2 = sc.add("tensor", lambda e, Pcur=Pcur, Bcur=Bcur: e.matmul(
                            ppow[:, 64:128], Pcur, Bcur, start=True, stop=True),
                            inc="pe")
                        pb_prev = sc.add("vector", lambda e, k=k: e.tensor_copy(
                            out=PB[k % 2][:], in_=ppow[:]), waits=[("pe", p2)], inc="v")
                        pb_ready = pb_prev
                        Pcur = PB[k % 2][:, 0:64]
                        Bcur = PB[k % 2][:, 64:128]
                if "solve" in trunc:
                    x_ready = sc.add("vector", lambda e, c=c: e.tensor_copy(
                        out=Xb[1][:], in_=Rsb[:, c * 256:(c + 1) * 256]),
                        waits=[("prep", prep_v)], inc="v")
                Xf = Xb[5 % 2][:]
                if "scan" in trunc:
                    ob_prev = sc.add("vector", lambda e, c=c: e.tensor_copy(
                        out=ob[64 * (c % 2):64 * (c % 2) + 64,
                               (c // 2) * 128:(c // 2 + 1) * 128],
                        in_=Xb[1][:, 0:128]), waits=[("v", x_ready)], inc="v")
                    continue
                tp2 = sc.add("tensor", lambda e, Xf=Xf: e.transpose(
                    ptr[:, 0:64], Xf[:, 128:256], id64[:]),
                    waits=[("v", x_ready), ("trc", sc.count("trc"))], inc="trp")
                kcf_t = sc.add("vector", lambda e: e.tensor_scalar(
                    out=kcf[:], in0=ptr[:, 0:64], scalar1=-1.0, scalar2=None,
                    op0=ALU.mult), waits=[("trp", tp2)], inc="trc")
                n1 = sc.add("tensor", lambda e, Xf=Xf: e.matmul(
                    pvn[:], id64[:], Xf[:, 0:128], start=True, stop=False),
                    waits=[("v", x_ready), ("v", S_ready[hv])], inc="pe")
                n2 = sc.add("tensor", lambda e, hv=hv: e.matmul(
                    pvn[:], kcf[:], Ssb[hv][:], start=False, stop=True),
                    waits=[("trc", kcf_t)], inc="pe")
                vn_t = sc.add("vector", lambda e: e.tensor_copy(
                    out=vnew[:], in_=pvn[:]), waits=[("pe", n2)], inc="v")
                o1_ = sc.add("tensor", lambda e, c=c, hv=hv: e.matmul(
                    pout[:], qg[:, c * C:(c + 1) * C], Ssb[hv][:],
                    start=True, stop=False),
                    waits=[("v", qg_t)] + ([("v", ob_prev)] if ob_prev else []),
                    inc="pe")
                o2_ = sc.add("tensor", lambda e, c=c: e.matmul(
                    pout[:], atn[:, c * C:(c + 1) * C], vnew[:],
                    start=False, stop=True), waits=[("v", vn_t)], inc="pe")
                ob_t = sc.add("vector", lambda e, c=c: e.tensor_copy(
                    out=ob[64 * (c % 2):64 * (c % 2) + 64,
                           (c // 2) * 128:(c // 2 + 1) * 128],
                    in_=pout[:]), waits=[("pe", o2_)], inc="v")
                ob_prev = ob_t
                d1 = sc.add("tensor", lambda e, c=c: e.matmul(
                    pdS[:], kdsb[:, c * 128:(c + 1) * 128], vnew[:],
                    start=True, stop=True), waits=[("v", vn_t)], inc="pe")
                S_ready[hv] = sc.add("vector", lambda e, hv=hv, cgl=cgl: e.scalar_tensor_tensor(
                    out=Ssb[hv][:], in0=Ssb[hv][:],
                    scalar=arep[:, hv * 64 + cgl:hv * 64 + cgl + 1],
                    in1=pdS[:], op0=ALU.mult, op1=ALU.add),
                    waits=[("pe", d1)], inc="v")
            pe_after_vh = sc.count("pe")
            zss = None
            for tl in range(TBS // 128):
                tp = sc.add("tensor", lambda e, tl=tl, hv=hv: e.transpose(
                    ptr[:], zs[:, hv * TBS + tl * 128:hv * TBS + (tl + 1) * 128],
                    idt[:]), waits=[("trc", sc.count("trc")), ("lv", lz)], inc="trp")
                zss = sc.add("scalar", lambda e, tl=tl: e.activation(
                    out=zt[:, tl * 128:(tl + 1) * 128], in_=ptr[:], func=AF.Silu),
                    waits=[("trp", tp)], inc="s")
                sc.add("gpsimd", lambda e: None, waits=[("s", zss)], inc="trc")
            sq2 = sc.add("scalar", lambda e: e.activation(out=obn[:], in_=ob[:],
                                                          func=AF.Square),
                         waits=[("v", sc.count("v"))], inc="s")
            r1 = sc.add("vector", lambda e: e.tensor_reduce(
                out=rsd[:].unsqueeze(2), in_=obn[:].rearrange("p (c j) -> p c j", j=128),
                axis=AX.X, op=ALU.add), waits=[("s", sq2)], inc="v")
            r2 = sc.add("vector", lambda e: e.tensor_scalar(
                out=rsd[:], in0=rsd[:], scalar1=1.0 / 128, scalar2=1e-6,
                op0=ALU.mult, op1=ALU.add), inc="v")
            r3 = sc.add("scalar", lambda e: e.activation(out=rsd[:], in_=rsd[:],
                                                         func=AF.Ln),
                        waits=[("v", r2)], inc="s")
            r4 = sc.add("scalar", lambda e: e.activation(out=rsd[:], in_=rsd[:],
                                                         func=AF.Exp, scale=-0.5),
                        inc="s")
            g1 = sc.add("vector", lambda e: e.tensor_tensor(
                out=obn[:].rearrange("p (c j) -> p c j", j=128),
                in0=ob[:].rearrange("p (c j) -> p c j", j=128),
                in1=rsd[:].unsqueeze(2).broadcast_to([128, TBS // 128, 128]),
                op=ALU.mult), waits=[("s", r4)], inc="v")
            g2 = sc.add("vector", lambda e: e.tensor_tensor(
                out=obn[:].rearrange("p (c j) -> p c j", j=128),
                in0=obn[:].rearrange("p (c j) -> p c j", j=128),
                in1=nwt[:].unsqueeze(1).broadcast_to([128, TBS // 128, 128]),
                op=ALU.mult), inc="v")
            g3 = sc.add("vector", lambda e: e.tensor_tensor(
                out=ob2[:], in0=obn[:], in1=zt[:], op=ALU.mult),
                waits=[("s", zss), ("st", sc.count("st"))], inc="v")
            for tl in range(TBS // 128):
                sc.dma("sync", og[t0 + tl * 128:t0 + (tl + 1) * 128,
                                  hv * 128:(hv + 1) * 128],
                       ob2[:, tl * 128:(tl + 1) * 128], waits=[("v", g3)], inc="st")
        tbd_t = sc.add("gpsimd", lambda e: None,
                       waits=[("st", sc.count("st")), ("v", sc.count("v")),
                              ("pe", sc.count("pe")), ("s", sc.count("s"))],
                       inc="tbd")
    sc.add("sync", lambda e: None, waits=[("st", sc.count("st"))])
    sc.build()
    ctx.close()
    return nc


def delta_launch(cqkv_b, ba_b, z_b, dt_bias, A_log, norm_weight):
    HPL = 4
    nc = cached(("delta", HPL), lambda: build_delta(HPL))
    C = 64
    lowi = np.tril(np.ones((64, 64), np.float32))
    slown = -np.tril(np.ones((64, 64), np.float32), -1)
    uppi = np.ascontiguousarray(np.triu(np.ones((64, 64), np.float32)))
    nwt = np.ascontiguousarray(np.tile(norm_weight.reshape(1, 128), (128, 1)).astype(BF))
    idn = np.eye(128, dtype=BF)
    onesf = np.ones((1, 128), np.float32)
    out = [np.zeros((S, 4096), dtype=BF) for _ in range(B)]
    for lc in range(32 // (2 * HPL)):
        maps = []
        for core in range(NCORES):
            b, half = divmod(core, 2)
            h0 = half * 16 + lc * HPL
            kh0 = h0 // 2
            bax = np.zeros((64, S), np.float32)
            bax[0:HPL] = ba_b[b][h0:h0 + HPL]
            bax[32:32 + HPL] = ba_b[b][32 + h0:32 + h0 + HPL]
            alb = np.zeros((64, 1), np.float32)
            alb[0:HPL, 0] = A_log[h0:h0 + HPL]
            alb[32:32 + HPL, 0] = dt_bias[h0:h0 + HPL]
            maps.append({
                "cq": np.ascontiguousarray(cqkv_b[b][kh0 * 128:(kh0 + HPL // 2) * 128]),
                "ck": np.ascontiguousarray(
                    cqkv_b[b][2048 + kh0 * 128:2048 + (kh0 + HPL // 2) * 128]),
                "cv": np.ascontiguousarray(
                    cqkv_b[b][4096 + h0 * 128:4096 + (h0 + HPL) * 128]),
                "z": np.ascontiguousarray(z_b[b][h0 * 128:(h0 + HPL) * 128]),
                "bax": bax, "albias": alb, "nwt_in": nwt, "lowi_in": lowi,
                "slown_in": slown, "uppi_in": uppi, "idn": idn, "onesf": onesf,
            })
        res = run(nc, maps)
        for core in range(NCORES):
            b, half = divmod(core, 2)
            h0 = half * 16 + lc * HPL
            out[b][:, h0 * 128:(h0 + HPL) * 128] = res[core]["og"]
    return out
